# version 21
# baseline (speedup 1.0000x reference)
import numpy as np
from contextlib import ExitStack

import concourse.bass as bass
import concourse.mybir as mybir
from concourse.bass_utils import run_bass_kernel_spmd

F32 = mybir.dt.float32
BF16 = mybir.dt.bfloat16
U8 = mybir.dt.uint8
AF = mybir.ActivationFunctionType
ALU = mybir.AluOpType

NCORES = 8
BPC = 4
SEQ = 2048
D = 1024
E2 = 2048
F1 = 4096
TPB = 512
NCH = 4
CH = 128
NPASS = BPC * (SEQ // TPB)
DKT = D // 128
EKT = E2 // 128
F1T = F1 // 128
CIW = TPB + 3
DBG = 0


class Ctr:
    __slots__ = ("h", "n")

    def __init__(self, h):
        self.h = h
        self.n = 0


class Slot:
    __slots__ = ("free",)

    def __init__(self):
        self.free = []

    def waits(self):
        return list(self.free)


def build_program():
    nc = bass.Bass("TRN2", target_bir_lowering=False)

    dx = nc.dram_tensor("x", [BPC * SEQ, D], F32, kind="ExternalInput").ap()
    dwinTr = nc.dram_tensor("winTr", [8, 128, 4096], F32, kind="ExternalInput").ap()
    dk0tr = nc.dram_tensor("k0tr", [8, 128, 4096], U8, kind="ExternalInput").ap()
    dk0br = nc.dram_tensor("k0br", [8, 128, 4096], U8, kind="ExternalInput").ap()
    dwoutTr = nc.dram_tensor("woutTr", [4, 128, 4096], F32, kind="ExternalInput").ap()
    dk1r = nc.dram_tensor("k1r", [32, 128, 4096], U8, kind="ExternalInput").ap()
    dk2r = nc.dram_tensor("k2r", [8, 128, 4096], U8, kind="ExternalInput").ap()
    dwxT = nc.dram_tensor("wxT", [E2, 32], F32, kind="ExternalInput").ap()
    dbtlog = nc.dram_tensor("btlog", [16, E2], F32, kind="ExternalInput").ap()
    dconvw = nc.dram_tensor("convw", [128, 64], F32, kind="ExternalInput").ap()
    dconvb = nc.dram_tensor("convb", [128, 16], F32, kind="ExternalInput").ap()
    ds012 = nc.dram_tensor("s012", [1, 3], F32, kind="ExternalInput").ap()
    dlng = nc.dram_tensor("lng_rep", [128, D], F32, kind="ExternalInput").ap()
    dlnb = nc.dram_tensor("lnb_rep", [128, D], F32, kind="ExternalInput").ap()
    dcsts = nc.dram_tensor("csts", [128, 4], F32, kind="ExternalInput").ap()
    dscst = nc.dram_tensor("scst", [1, 384], F32, kind="ExternalInput").ap()
    didn = nc.dram_tensor("idn", [128, 128], F32, kind="ExternalInput").ap()
    dout = nc.dram_tensor("o", [BPC * SEQ, D], F32, kind="ExternalOutput").ap()

    winTs = nc.dram_tensor("winTs", [8, 128, 4096], BF16, kind="Internal").ap()
    w0s = nc.dram_tensor("w0s", [8, 128, 4096], BF16, kind="Internal").ap()
    w1s = nc.dram_tensor("w1s", [32, 128, 4096], BF16, kind="Internal").ap()
    woutTs = nc.dram_tensor("woutTs", [4, 128, 4096], BF16, kind="Internal").ap()
    w2s = nc.dram_tensor("w2s", [8, 128, 4096], BF16, kind="Internal").ap()

    with ExitStack() as stack:
        e = stack.enter_context
        sbt = lambda name, shape, dt: e(nc.sbuf_tensor(name, shape, dt))
        id_sb = sbt("id_sb", [128, 128], F32)
        csts = sbt("csts_sb", [128, 4], F32)
        scst = sbt("scst_sb", [1, 384], F32)
        s012 = sbt("s012_sb", [1, 3], F32)
        wxT = sbt("wxT_sb", [128, EKT * 32], BF16)
        bt = sbt("bt_sb", [16, E2], BF16)
        convw = sbt("convw_sb", [128, 64], F32)
        convb = sbt("convb_sb", [128, 16], F32)
        lng = sbt("lng_sb", [128, D], F32)
        lnb = sbt("lnb_sb", [128, D], F32)
        scales = sbt("scales_sb", [128, 3], F32)
        decay = sbt("decay_sb", [128, TPB], F32)
        state = sbt("state_sb", [128, EKT], F32)
        hist = sbt("hist_sb", [128, EKT * 3], BF16)
        x_sb = sbt("x_sb", [128, NCH * D], F32)
        xntok = sbt("xntok_sb", [128, NCH * D], F32)
        xnfm = sbt("xnfm_sb", [128, DKT * TPB], BF16)
        a0 = sbt("a0_sb", [128, F1T * TPB], BF16)
        a1 = sbt("a1_sb", [128, F1T * TPB], BF16)
        h_sb = sbt("h_sb", [128, 2 * TPB], F32)
        hct = sbt("hct_sb", [128, TPB], F32)
        acc = sbt("acc_sb", [128, 2 * TPB], F32)
        sC = sbt("sC_sb", [16, TPB], BF16)
        stok = sbt("stok_sb", [128, NCH * D], BF16)
        resid = sbt("resid_sb", [128, 2 * D], F32)
        snorm = sbt("snorm_sb", [128, D], F32)
        g_sb = sbt("g_sb", [128, 2 * D], BF16)
        out_sb = sbt("out_sb", [128, 2 * D], F32)
        bn12 = sbt("bn12_sb", [128, 12], F32)
        bn2 = sbt("bn2_sb", [128, 2], F32)
        lnv = sbt("lnv_sb", [128, 1], F32)
        rstA = sbt("rstA_sb", [128, NCH], F32)
        nmrA = sbt("nmrA_sb", [128, NCH], F32)
        rstD = sbt("rstD_sb", [128, 1], F32)
        nmrD = sbt("nmrD_sb", [128, 1], F32)
        y_sb = sbt("y_sb", [128, EKT * TPB], BF16)
        wstr = sbt("wstr_sb", [128, 3 * 4096], BF16)
        pd = [e(nc.psum_tensor(f"pd{i}", [128, 1024], F32)) for i in range(4)]

        mksem = lambda name: Ctr(e(nc.semaphore(name)))
        svv = mksem("svv")
        sss = mksem("sss")
        spe = mksem("spe")
        wch = [mksem(f"wch{i}") for i in range(3)]
        xch = [mksem(f"xch{i}") for i in range(NCH)]
        x2c = [mksem(f"x2c{i}") for i in range(NCH)]
        och = [mksem(f"och{i}") for i in range(2)]
        gwc = [mksem(f"gwc{i}") for i in range(3)]
        gsk = mksem("gsk")
        wbk = {f: mksem(f"wbk{f}") for f in "BEDFG"}
        gmisc = mksem("gmisc")

        block = e(nc.Block())
        Q = {k: [] for k in "TVSYG"}

        def mk_dma(eng, ctr):
            def emit(build, waits=()):
                w = [(c.h, v) for (c, v) in waits if v > 0]
                ctr.n += 16
                hh0 = ctr.h

                def fn(engine):
                    for hh, vv_ in w:
                        engine.wait_ge(hh, vv_)
                    build(engine).then_inc(hh0, 16)
                Q[eng].append(fn)
                return ctr.n
            return emit

        def mk_serial(eng, ctr):
            def emit(build, waits=()):
                prev = ctr.n
                w = [(c.h, v) for (c, v) in waits if v > 0]
                ctr.n += 1
                hh0 = ctr.h

                def fn(engine):
                    if prev > 0:
                        engine.wait_ge(hh0, prev)
                    for hh, vv_ in w:
                        engine.wait_ge(hh, vv_)
                    build(engine).then_inc(hh0, 1)
                Q[eng].append(fn)
                return ctr.n
            return emit

        V = mk_serial("V", svv)
        S = mk_serial("S", sss)
        YDc = [mk_dma("Y", c) for c in wch]
        SDc = [mk_dma("S", c) for c in xch]
        X2c = [mk_dma("S", c) for c in x2c]
        ODc = [mk_dma("S", c) for c in och]
        GW = [mk_dma("G", c) for c in gwc]
        EY = [mk_dma("Y", c) for c in wch]
        EG = [mk_dma("G", c) for c in wch]
        GSK = mk_dma("G", gsk)
        WB = {f: mk_dma("Y", c) for f, c in wbk.items()}
        GD = mk_dma("G", gmisc)

        def T(builds, waits=()):
            w = [(c.h, v) for (c, v) in waits if v > 0]
            spe.n += 1
            hh0 = spe.h

            def fn(t):
                for hh, vv_ in w:
                    t.wait_ge(hh, vv_)
                inst = None
                for b in builds:
                    inst = b(t)
                inst.then_inc(hh0, 1)
            Q["T"].append(fn)
            return spe.n

        def yv(dt, c=None):
            if c is None:
                return y_sb[:, dt * TPB:(dt + 1) * TPB]
            return y_sb[:, dt * TPB + c * CH: dt * TPB + (c + 1) * CH]

        def zv(dt):
            return a0[:, EKT * TPB + dt * TPB: EKT * TPB + (dt + 1) * TPB]

        def civ(dt, lo, hi):
            return a1[:, dt * CIW + lo: dt * CIW + hi]

        def a1t(k, c):
            return a1[:, k * TPB + c * CH: k * TPB + (c + 1) * CH]

        def wslot(j):
            return wstr[:, j * 4096:(j + 1) * 4096]

        def wslot_kt(j, kt):
            return wstr[:, j * 4096:(j + 1) * 4096].rearrange(
                "p (kt m) -> p kt m", kt=kt)

        xnfm_v = xnfm[:].rearrange("p (kt t) -> p kt t", kt=DKT)
        civ_all = a1[:, 0:EKT * CIW].rearrange("p (dt w) -> p dt w", w=CIW)
        hist_v = hist[:].rearrange("p (dt w) -> p dt w", w=3)
        pd2_kt = pd[2][:].rearrange("p (kt m) -> p kt m", kt=DKT)

        rotAB = [Slot() for _ in range(4)]
        rotC = [Slot() for _ in range(2)]
        slot3 = Slot()
        pdFree = [Slot() for _ in range(4)]
        wstrS = [Slot() for _ in range(3)]
        accS = [Slot() for _ in range(2)]
        gS = [Slot() for _ in range(2)]
        outS = [Slot() for _ in range(2)]
        residS = [Slot() for _ in range(2)]
        xchunk = [Slot() for _ in range(NCH)]
        xn_tok_free = [Slot() for _ in range(NCH)]
        xnfm_free = Slot()
        a1_freeV = Slot()
        outW = [None, None]

        def rotab_pd(m):
            j = m % 4
            return (pd[j // 2][:, (j % 2) * 512:(j % 2) * 512 + 512], rotAB[j])

        WFAM = {"B": (dwinTr, winTs), "E": (dk0tr, w0s), "D": (dwoutTr, woutTs),
                "F": (dk1r, w1s), "G": (dk2r, w2s)}
        WBTOT = {"B": 8 * 16, "E": 8 * 16, "D": 4 * 16, "F": 32 * 16, "G": 8 * 16}
        k0sub_prev = [0]
        wbprev = {f: 0 for f in "BEDFG"}
        gslot = [0]

        def wload(P, fam, idx, first):
            gi = gslot[0]
            gslot[0] += 1
            sl = gi % 3
            orig, scratch = WFAM[fam]
            if P >= 1:
                w = wstrS[sl].waits()
                if P == 1 and first:
                    w = w + [(wbk[fam], WBTOT[fam])]
                emit_q = EY[sl] if gi % 2 == 0 else EG[sl]
                lw = emit_q(lambda y, sl=sl, scratch=scratch, idx=idx:
                            y.dma_start(wslot(sl), scratch[idx]), waits=w)
                return sl, [(wch[sl], lw)], []
            lw = GW[sl](lambda g, sl=sl, orig=orig, idx=idx:
                        g.dma_start(wslot(sl), orig[idx]), waits=wstrS[sl].waits())
            if fam == "E":
                lb = GSK(lambda g, idx=idx: g.dma_start(stok[:], dk0br[idx]),
                         waits=([(svv, k0sub_prev[0])] if k0sub_prev[0] else []))
                tv = V(lambda v, sl=sl: v.tensor_tensor(
                    wslot(sl), wslot(sl), stok[:], ALU.subtract),
                    waits=[(gwc[sl], lw), (gsk, lb)])
                k0sub_prev[0] = tv
                mmdep = [(svv, tv)]
            elif fam in ("F", "G"):
                tv = V(lambda v, sl=sl: v.tensor_scalar(
                    wslot(sl), wslot(sl), 2.0, -1.0, ALU.mult, ALU.add),
                    waits=[(gwc[sl], lw)])
                mmdep = [(svv, tv)]
            else:
                mmdep = [(gwc[sl], lw)]
            wprev = wbprev[fam]
            wb = WB[fam](lambda y, scratch=scratch, idx=idx, sl=sl:
                         y.dma_start(scratch[idx], wslot(sl)),
                         waits=mmdep + ([(wbk[fam], wprev)] if wprev else []))
            wbprev[fam] = wb
            return sl, mmdep, [(wbk[fam], wb)]

        GD(lambda g: g.dma_start(id_sb[:], didn))
        GD(lambda g: g.dma_start(csts[:], dcsts))
        GD(lambda g: g.dma_start(scst[:], dscst))
        ld_s012 = GD(lambda g: g.dma_start(s012[:], ds012))
        GD(lambda g: g.dma_start(convw[:], dconvw))
        GD(lambda g: g.dma_start(convb[:], dconvb))
        GD(lambda g: g.dma_start(lng[:], dlng))
        GD(lambda g: g.dma_start(lnb[:], dlnb))
        GD(lambda g: g.dma_start(
            wxT[:].rearrange("p (dt j) -> p dt j", dt=EKT),
            dwxT.rearrange("(dt p) j -> p dt j", p=128)))
        ld_bt = GD(lambda g: g.dma_start(out_sb[0:16, 0:E2], dbtlog))
        V(lambda v: v.memset(decay[:], 0.95))
        bt_done = S(lambda s: s.activation(bt[:], out_sb[0:16, 0:E2], AF.Exp),
                    waits=[(gmisc, ld_bt)])
        t_sc = T([(lambda t, i=i: t.matmul(pd[2][:, i:i + 1],
                                           scst[0:1, i * 128:(i + 1) * 128],
                                           s012[0:1, i:i + 1], start=True, stop=True))
                  for i in range(3)],
                 waits=[(gmisc, ld_bt)])
        sc_done = V(lambda v: v.tensor_copy(scales[:], pd[2][:, 0:3]), waits=[(spe, t_sc)])
        rotC[0].free = [(svv, sc_done)]
        rotC[1].free = [(svv, sc_done)]


        def emit_A_loads(P):
            row0 = P * TPB
            lxs = []
            for c in range(NCH):
                lxs.append(SDc[c](lambda s, c=c, r0=row0: s.dma_start(
                    x_sb[:, c * D:(c + 1) * D], dx[r0 + c * CH: r0 + (c + 1) * CH, :]),
                    waits=xchunk[c].waits()))
            return lxs

        def emit_A_stats(lxs):
            vals = []
            for c in range(NCH):
                V(lambda v, c=c: v.bn_stats(bn12[:, 0:6], x_sb[:, c * D:c * D + 512]),
                  waits=[(xch[c], lxs[c])])
                V(lambda v, c=c: v.bn_stats(bn12[:, 6:12], x_sb[:, c * D + 512:(c + 1) * D]))
                v2 = V(lambda v: v.bn_aggr(bn2[:], bn12[:]))
                S(lambda s: s.activation(lnv[:], bn2[:, 1:2], AF.Ln, bias=csts[:, 0:1]),
                  waits=[(svv, v2)])
                s2 = S(lambda s, c=c: s.activation(rstA[:, c:c + 1], lnv[:], AF.Exp,
                                                   scale=csts[:, 1:2]))
                v3 = V(lambda v, c=c: v.tensor_scalar(nmrA[:, c:c + 1], bn2[:, 0:1],
                                                      rstA[:, c:c + 1], -1.0,
                                                      ALU.mult, ALU.mult),
                       waits=[(sss, s2)])
                vals.append(v3)
            return vals

        def emit_A_s3(c, vals):
            s3 = S(lambda s, c=c: s.activation(
                xntok[:, c * D:(c + 1) * D], x_sb[:, c * D:(c + 1) * D], AF.Identity,
                scale=rstA[:, c:c + 1], bias=nmrA[:, c:c + 1]),
                waits=[(svv, vals[c])] + xn_tok_free[c].waits())
            xchunk[c].free = [(sss, s3)]
            return s3

        def emit_A_transp(c, s3):
            tg = T([(lambda t, c=c, j=j: t.transpose(
                pd[2][:, j * 128:(j + 1) * 128],
                xntok[:, c * D + j * 128: c * D + (j + 1) * 128], id_sb[:]))
                for j in range(DKT)],
                waits=[(sss, s3)] + slot3.waits())
            xn_ready = S(lambda s, c=c: s.activation(
                xnfm_v[:, :, c * CH:(c + 1) * CH], pd2_kt, AF.Identity),
                waits=[(spe, tg)] + (xnfm_free.waits() if c == 0 else []))
            slot3.free = [(sss, xn_ready)]
            return xn_ready

        def emit_A_apply(vals):
            for c in range(NCH):
                xn_ready = emit_A_transp(c, emit_A_s3(c, vals))
            return xn_ready

        slot3.free = [(svv, sc_done)]
        xn_ready = emit_A_apply(emit_A_stats(emit_A_loads(0)))
        V(lambda v: v.memset(hist[:], 0.0))
        hci_cur = V(lambda v: v.tensor_copy(civ_all[:, :, 0:3], hist_v))
        pending_st = []
        for P in range(NPASS):
            batch_start = (P % (SEQ // TPB) == 0)
            row0 = P * TPB
            lxs_next = emit_A_loads(P + 1) if P + 1 < NPASS else None

            hci = hci_cur
            convs = [0] * EKT
            zsilu_last = 0
            pend = {}
            for m in range(32):
                sb_i = m // 4
                j = m % 4
                if j == 0:
                    sl, mmw, fre = wload(P, "B", sb_i, first=(sb_i == 0))
                pslot, prot = rotab_pd(m)
                tg = T([(lambda t, sl=sl, j=j, k=k, pslot=pslot: t.matmul(
                    pslot, wslot_kt(sl, DKT)[:, k, j * 128:(j + 1) * 128],
                    xnfm_v[:, k, :],
                    start=(k == 0), stop=(k == DKT - 1)))
                    for k in range(DKT)],
                    waits=mmw + [(sss, xn_ready)] + prot.waits())
                if j == 3:
                    wstrS[sl].free = [(spe, tg)] + fre
                if m - 2 in pend:
                    m2, asl2, ho2 = pend.pop(m - 2)
                    convs[m2] = S(lambda s, m2=m2, asl2=asl2: s.activation(
                        civ(m2, 3, CIW), acc[:, asl2 * TPB:(asl2 + 1) * TPB], AF.Silu),
                        waits=[(svv, ho2)])
                    accS[asl2].free = [(sss, convs[m2])]
                if 1 <= m <= 2 and pending_st:
                    pending_st.pop(0)()
                if m < EKT:
                    ev = S(lambda s, m=m, pslot=pslot: s.copy(civ(m, 3, CIW), pslot),
                           waits=[(spe, tg), (svv, hci)])
                    prot.free = [(sss, ev)]
                    asl = m % 2
                    V(lambda v, m=m, asl=asl: v.tensor_scalar(
                        acc[:, asl * TPB:(asl + 1) * TPB], civ(m, 0, TPB),
                        convw[:, m * 4: m * 4 + 1], convb[:, m:m + 1], ALU.mult, ALU.add),
                        waits=[(sss, ev)] + accS[asl].waits())
                    for kk in range(1, 4):
                        V(lambda v, m=m, asl=asl, kk=kk: v.scalar_tensor_tensor(
                            acc[:, asl * TPB:(asl + 1) * TPB], civ(m, kk, TPB + kk),
                            convw[:, m * 4 + kk: m * 4 + kk + 1],
                            acc[:, asl * TPB:(asl + 1) * TPB], ALU.mult, ALU.add))
                    ho = V(lambda v, m=m: v.tensor_copy(
                        hist[:, m * 3:(m + 1) * 3], civ(m, TPB, CIW)))
                    pend[m] = (m, asl, ho)
                else:
                    zsilu_last = S(lambda s, m=m, pslot=pslot: s.activation(
                        zv(m - EKT), pslot, AF.Silu),
                        waits=[(spe, tg)])
                    prot.free = [(sss, zsilu_last)]
            tg_sp = T([(lambda t, k=k: t.matmul(
                pd[2][0:32, 0:512], wxT[:, k * 32:(k + 1) * 32], civ(k, 3, CIW),
                start=(k == 0), stop=(k == EKT - 1)))
                for k in range(EKT)],
                waits=[(sss, convs[EKT - 1]), (sss, bt_done)] + rotC[0].waits())
            sC_ev = S(lambda s: s.copy(sC[:], pd[2][0:16, 0:512]), waits=[(spe, tg_sp)])
            rotC[0].free = [(sss, sC_ev)]

            y_vals = [0] * EKT
            a0_last = 0
            lw = 0
            sl = 0
            for i in range(16):
                dt = i
                hsl = dt % 2
                init = 0.0 if batch_start else state[:, dt:dt + 1]
                V(lambda v, dt=dt, hsl=hsl, init=init: v.tensor_tensor_scan(
                    h_sb[:, hsl * TPB:(hsl + 1) * TPB], decay[:], civ(dt, 3, CIW),
                    init, ALU.mult, ALU.add),
                    waits=[(sss, convs[dt])])
                V(lambda v, dt=dt, hsl=hsl: v.tensor_copy(
                    state[:, dt:dt + 1], h_sb[:, hsl * TPB + TPB - 1: hsl * TPB + TPB]))
                csl = dt % 2
                cps = pd[2][:, csl * 512:(csl + 1) * 512]
                tg_c = T([(lambda t, dt=dt, cps=cps: t.matmul(
                    cps, bt[:, dt * 128:(dt + 1) * 128], sC[:], start=True, stop=True))],
                    waits=[(sss, sC_ev)] + rotC[csl].waits())
                V(lambda v, dt=dt, hsl=hsl: v.tensor_tensor(
                    hct[:], h_sb[:, hsl * TPB:(hsl + 1) * TPB], zv(dt), ALU.mult),
                    waits=([(sss, zsilu_last)] if dt == 0 else []))
                y_vals[dt] = V(lambda v, dt=dt, cps=cps: v.tensor_tensor(
                    yv(dt), hct[:], cps, ALU.mult),
                    waits=[(spe, tg_c)])
                rotC[csl].free = [(svv, y_vals[dt])]
                for m in (2 * i, 2 * i + 1):
                    sb_i = m // 4
                    j = m % 4
                    if j == 0:
                        sl, mmw, fre = wload(P, "E", sb_i, first=(sb_i == 0))
                    pslot, prot = rotab_pd(m)
                    tg = T([(lambda t, sl=sl, j=j, k=k, pslot=pslot: t.matmul(
                        pslot, wslot_kt(sl, DKT)[:, k, j * 128:(j + 1) * 128],
                        xnfm_v[:, k, :],
                        start=(k == 0), stop=(k == DKT - 1)))
                        for k in range(DKT)],
                        waits=mmw + prot.waits())
                    if j == 3:
                        wstrS[sl].free = [(spe, tg)] + fre
                    extra = [(svv, y_vals[m - EKT])] if m >= EKT else []
                    a0_last = S(lambda s, m=m, pslot=pslot: s.activation(
                        a0[:, m * TPB:(m + 1) * TPB], pslot, AF.Relu, scale=scales[:, 0:1]),
                        waits=[(spe, tg)] + extra)
                    prot.free = [(sss, a0_last)]
            tg_e_last = tg
            xnfm_free.free = [(spe, tg_e_last)]
            x2v = [0] * NCH
            x2v[2] = X2c[2](lambda v, r0=row0: v.dma_start(
                h_sb[:], dx[r0 + 2 * CH: r0 + 3 * CH, :]),
                waits=[(svv, y_vals[EKT - 1])])
            x2v[3] = X2c[3](lambda v, r0=row0: v.dma_start(
                acc[:], dx[r0 + 3 * CH: r0 + 4 * CH, :]),
                waits=[(sss, convs[EKT - 1])])
            a_vals = emit_A_stats(lxs_next) if lxs_next is not None else None

            for k in range(EKT):
                sb_i = k // 4
                j = k % 4
                if j == 0:
                    sl, mmw, fre = wload(P, "D", sb_i, first=(sb_i == 0))
                extra = [(svv, y_vals[EKT - 1]), (sss, a0_last)] if k == 0 else []
                builds = []
                for c in range(NCH):
                    for hh in range(2):
                        builds.append(lambda t, k=k, c=c, hh=hh, sl=sl, j=j: t.matmul(
                            pd[c][:, hh * 512:(hh + 1) * 512], yv(k, c),
                            wslot(sl)[:, j * 1024 + hh * 512: j * 1024 + (hh + 1) * 512],
                            start=(k == 0), stop=(k == EKT - 1), skip_group_check=True))
                tg_d = T(builds, waits=mmw + extra)
                if j == 3:
                    wstrS[sl].free = [(spe, tg_d)] + fre
            resid_vals = [0] * NCH
            rloc = [resid[:, 0:D], resid[:, D:2 * D], out_sb[:, 0:D], out_sb[:, D:2 * D]]
            for c in range(NCH):
                ww = residS[c].waits() if c < 2 else list(outS[c - 2].waits())
                rv = V(lambda v, c=c: v.tensor_tensor(
                    rloc[c], pd[c][:], xntok[:, c * D:(c + 1) * D], ALU.add),
                    waits=[(spe, tg_d)] + ww)
                resid_vals[c] = rv
                pdFree[c].free = [(svv, rv)]
                xn_tok_free[c].free = [(svv, rv)]

            slot3.free = [(svv, resid_vals[2])]
            if P + 1 < NPASS:
                s3_next = [emit_A_s3(c, a_vals) for c in range(NCH)]

            s3_c0 = 0
            ln_last = 0
            for c in range(NCH):
                V(lambda v, c=c: v.bn_stats(bn12[:, 0:6], rloc[c][:, 0:512]))
                V(lambda v, c=c: v.bn_stats(bn12[:, 6:12], rloc[c][:, 512:D]))
                v2 = V(lambda v: v.bn_aggr(bn2[:], bn12[:]))
                S(lambda s: s.activation(lnv[:], bn2[:, 1:2], AF.Ln, bias=csts[:, 0:1]),
                  waits=[(svv, v2)])
                s2 = S(lambda s: s.activation(rstD[:], lnv[:], AF.Exp, scale=csts[:, 1:2]))
                v3 = V(lambda v: v.tensor_scalar(nmrD[:], bn2[:, 0:1], rstD[:], -1.0,
                                                 ALU.mult, ALU.mult),
                       waits=[(sss, s2)])
                s3 = S(lambda s, c=c: s.activation(
                    snorm[:], rloc[c], AF.Identity, scale=rstD[:], bias=nmrD[:]),
                    waits=[(svv, v3)])
                if c == 0:
                    s3_c0 = s3
                if c < 2:
                    residS[c].free = [(sss, s3)]
                else:
                    outW[c - 2] = (sss, s3)
                V(lambda v: v.tensor_tensor(snorm[:], snorm[:], lng[:], ALU.mult),
                  waits=[(sss, s3)])
                ln_last = V(lambda v, c=c: v.tensor_tensor(
                    stok[:, c * D:(c + 1) * D], snorm[:], lnb[:], ALU.add))

            a1_vals = [0] * 32
            for m in range(32):
                sl, mmw, fre = wload(P, "F", m, first=(m == 0))
                pslot, prot = rotab_pd(m)
                pw = list(prot.waits())
                if m < 4:
                    pw += pdFree[[0, 0, 1, 1][m]].waits()
                tg = T([(lambda t, sl=sl, k=k, pslot=pslot: t.matmul(
                    pslot, wslot_kt(sl, F1T)[:, k, :], a0[:, k * TPB:(k + 1) * TPB],
                    start=(k == 0), stop=(k == F1T - 1)))
                    for k in range(F1T)],
                    waits=mmw + [(sss, a0_last)] + pw)
                wstrS[sl].free = [(spe, tg)] + fre
                extra = [(svv, y_vals[EKT - 1])] if m == 0 else []
                a1_vals[m] = S(lambda s, m=m, pslot=pslot: s.activation(
                    a1[:, m * TPB:(m + 1) * TPB], pslot, AF.Relu, scale=scales[:, 1:2]),
                    waits=[(spe, tg)] + extra)
                prot.free = [(sss, a1_vals[m])]
                if P + 1 < NPASS and m in (3, 7, 11, 15):
                    xn_ready = emit_A_transp((m - 3) // 4, s3_next[(m - 3) // 4])
                if m == 8:
                    x2v[0] = X2c[0](lambda s, r0=row0: s.dma_start(
                        snorm[:], dx[r0: r0 + CH, :]), waits=[(svv, ln_last)])
                elif m == 9:
                    x2v[1] = X2c[1](lambda s, r0=row0: s.dma_start(
                        resid[:, 0:D], dx[r0 + CH: r0 + 2 * CH, :]),
                        waits=[(sss, s3_c0)])

            for k in range(F1T):
                sb_i = k // 4
                j = k % 4
                if j == 0:
                    sl, mmw, fre = wload(P, "G", sb_i, first=(sb_i == 0))
                extra = [(sss, a1_vals[31]), (svv, resid_vals[3])] if k == 0 else []
                if k == 0 and P + 1 < NPASS:
                    extra = extra + [(sss, xn_ready)]
                builds = []
                for c in range(NCH):
                    for hh in range(2):
                        builds.append(lambda t, k=k, c=c, hh=hh, sl=sl, j=j: t.matmul(
                            pd[c][:, hh * 512:(hh + 1) * 512], a1t(k, c),
                            wslot(sl)[:, j * 1024 + hh * 512: j * 1024 + (hh + 1) * 512],
                            start=(k == 0), stop=(k == F1T - 1), skip_group_check=True))
                tg_g = T(builds, waits=mmw + extra)
                if j == 3:
                    wstrS[sl].free = [(spe, tg_g)] + fre
                if k == 16 and P + 1 < NPASS:
                    if (P + 1) % (SEQ // TPB) == 0:
                        V(lambda v: v.memset(hist[:], 0.0), waits=[(spe, tg_g)])
                        hci_cur = V(lambda v: v.tensor_copy(civ_all[:, :, 0:3], hist_v))
                    else:
                        hci_cur = V(lambda v: v.tensor_copy(civ_all[:, :, 0:3], hist_v),
                                    waits=[(spe, tg_g)])
            gvs = [0] * NCH
            for c in (0, 1):
                gvs[c] = S(lambda s, c=c: s.activation(
                    g_sb[:, (c % 2) * D:(c % 2 + 1) * D], pd[c][:], AF.Silu,
                    scale=scales[:, 2:3]),
                    waits=[(spe, tg_g)] + gS[c % 2].waits())
            rotAB[0].free = [(sss, gvs[0])]
            rotAB[1].free = [(sss, gvs[0])]
            rotAB[2].free = [(sss, gvs[1])]
            rotAB[3].free = [(sss, gvs[1])]
            for c in range(NCH):
                gsl = c % 2
                if c >= 2:
                    gvs[c] = S(lambda s, c=c, gsl=gsl: s.activation(
                        g_sb[:, gsl * D:(gsl + 1) * D], pd[c][:], AF.Silu,
                        scale=scales[:, 2:3]),
                        waits=[(spe, tg_g)] + gS[gsl].waits())
                    if c == 2:
                        rotC[0].free = [(sss, gvs[2])]
                        rotC[1].free = [(sss, gvs[2])]
                gv = gvs[c]
                osl = c % 2
                oww = [outW[osl]] if outW[osl] else []
                if DBG == 6:
                    u1 = V(lambda v, osl=osl: v.memset(
                        out_sb[:, osl * D:(osl + 1) * D], 7.0),
                        waits=[(sss, gv)] + outS[osl].waits() + oww)
                    u2 = u1
                elif DBG == 1:
                    u1 = V(lambda v, c=c, osl=osl: v.tensor_copy(
                        out_sb[:, osl * D:(osl + 1) * D], xntok[:, c * D:(c + 1) * D]),
                        waits=[(sss, gv)] + outS[osl].waits() + oww)
                    u2 = u1
                elif DBG == 3:
                    u1 = V(lambda v, c=c, osl=osl: v.tensor_copy(
                        out_sb[:, osl * D:(osl + 1) * D], stok[:, c * D:(c + 1) * D]),
                        waits=[(sss, gv)] + outS[osl].waits() + oww)
                    u2 = u1
                elif DBG == 4:
                    u1 = V(lambda v, gsl=gsl, osl=osl: v.tensor_copy(
                        out_sb[:, osl * D:(osl + 1) * D], g_sb[:, gsl * D:(gsl + 1) * D]),
                        waits=[(sss, gv)] + outS[osl].waits() + oww)
                    u2 = u1
                else:
                    xsrc = [snorm[:], resid[:, 0:D], h_sb[:], acc[:]][c]
                    u1 = V(lambda v, c=c, gsl=gsl, osl=osl: v.tensor_tensor(
                        out_sb[:, osl * D:(osl + 1) * D], stok[:, c * D:(c + 1) * D],
                        g_sb[:, gsl * D:(gsl + 1) * D], ALU.mult),
                        waits=[(sss, gv)] + outS[osl].waits() + oww)
                    u2 = V(lambda v, osl=osl, xsrc=xsrc: v.tensor_tensor(
                        out_sb[:, osl * D:(osl + 1) * D], out_sb[:, osl * D:(osl + 1) * D],
                        xsrc, ALU.add),
                        waits=[(x2c[c], x2v[c])])
                outW[osl] = None
                gS[gsl].free = [(svv, u1)]

                def emit_store(c=c, osl=osl, r0=row0, u2=u2):
                    ov = ODc[osl](lambda s, c=c, osl=osl, r0=r0: s.dma_start(
                        dout[r0 + c * CH: r0 + (c + 1) * CH, :],
                        out_sb[:, osl * D:(osl + 1) * D]),
                        waits=[(svv, u2)])
                    outS[osl].free = [(och[osl], ov)]
                if c < 2:
                    emit_store()
                else:
                    pending_st.append(emit_store)
            a1_freeV.free = [(svv, y_vals[EKT - 1])]

        for st in pending_st:
            st()
        pending_st.clear()

        def fin(s):
            s.wait_ge(och[0].h, och[0].n)
            s.wait_ge(och[1].h, och[1].n)
        Q["S"].append(fin)

        @block.tensor
        def _(t):
            for fn in Q["T"]:
                fn(t)

        @block.vector
        def _(v):
            for fn in Q["V"]:
                fn(v)

        @block.scalar
        def _(s):
            for fn in Q["S"]:
                fn(s)

        @block.sync
        def _(y):
            for fn in Q["Y"]:
                fn(y)

        @block.gpsimd
        def _(g):
            for fn in Q["G"]:
                fn(g)

    return nc


_CACHE = {}


def kernel(L0_attn_x, L0_attn_W_in, L0_attn_conv_w, L0_attn_conv_b, L0_attn_W_x,
           L0_attn_B_log, L0_attn_W_out, L0_attn_ln_g, L0_attn_ln_b,
           L0_ffn_k0, L0_ffn_k1, L0_ffn_k2, L0_ffn_s0, L0_ffn_s1, L0_ffn_s2):
    x = np.ascontiguousarray(np.asarray(L0_attn_x, np.float32))
    W_in = np.asarray(L0_attn_W_in, np.float32)
    conv_w = np.asarray(L0_attn_conv_w, np.float32)
    conv_b = np.asarray(L0_attn_conv_b, np.float32)
    W_x = np.asarray(L0_attn_W_x, np.float32)
    B_log = np.asarray(L0_attn_B_log, np.float32)
    W_out = np.asarray(L0_attn_W_out, np.float32)
    ln_g = np.asarray(L0_attn_ln_g, np.float32)
    ln_b = np.asarray(L0_attn_ln_b, np.float32)
    k0 = np.ascontiguousarray(np.asarray(L0_ffn_k0)).view(np.uint8)
    k1 = np.ascontiguousarray(np.asarray(L0_ffn_k1)).view(np.uint8)
    k2 = np.ascontiguousarray(np.asarray(L0_ffn_k2)).view(np.uint8)

    winT = W_in.T
    winTr = np.ascontiguousarray(
        winT.reshape(8, 128, 8, 512).transpose(2, 1, 0, 3).reshape(8, 128, 4096))
    k0tr = np.ascontiguousarray(
        k0[:1024].reshape(8, 128, 8, 512).transpose(2, 1, 0, 3).reshape(8, 128, 4096))
    k0br = np.ascontiguousarray(
        k0[1024:].reshape(8, 128, 8, 512).transpose(2, 1, 0, 3).reshape(8, 128, 4096))
    woutTr = np.ascontiguousarray(
        W_out.T.reshape(4, 4, 128, 1024).transpose(0, 2, 1, 3).reshape(4, 128, 4096))
    k1r = np.ascontiguousarray(
        k1.reshape(32, 128, 32, 128).transpose(2, 1, 0, 3).reshape(32, 128, 4096))
    k2r = np.ascontiguousarray(
        k2.reshape(8, 4, 128, 1024).transpose(0, 2, 1, 3).reshape(8, 128, 4096))
    wxT = np.ascontiguousarray(
        np.concatenate([W_x[16:32], W_x[0:16]], axis=0).T)
    btlog = np.ascontiguousarray(B_log.T)
    convw = np.ascontiguousarray(
        conv_w[:, 0, :].reshape(16, 128, 4).transpose(1, 0, 2).reshape(128, 64))
    convb = np.ascontiguousarray(conv_b.reshape(16, 128).T)
    s012 = np.stack([np.asarray(L0_ffn_s0, np.float32).reshape(()),
                     np.asarray(L0_ffn_s1, np.float32).reshape(()),
                     np.asarray(L0_ffn_s2, np.float32).reshape(())]).reshape(1, 3)
    lng_rep = np.ascontiguousarray(np.broadcast_to(ln_g[None, :], (128, 1024)))
    lnb_rep = np.ascontiguousarray(np.broadcast_to(ln_b[None, :], (128, 1024)))
    csts = np.ascontiguousarray(np.broadcast_to(
        np.array([[1e-5, -0.5, 0.0, 0.0]], np.float32), (128, 4)))
    scst = np.zeros((1, 384), np.float32)
    scst[0, 0:128] = 4.0
    scst[0, 128:256] = 2.0
    scst[0, 256:384] = 1.0
    idn = np.eye(128, dtype=np.float32)

    key = ("nc", DBG)
    if key not in _CACHE:
        _CACHE[key] = build_program()
    nc = _CACHE[key]

    shared = dict(winTr=winTr, k0tr=k0tr, k0br=k0br, woutTr=woutTr, k1r=k1r,
                  k2r=k2r, wxT=wxT, btlog=btlog, convw=convw,
                  convb=convb, s012=s012, lng_rep=lng_rep,
                  lnb_rep=lnb_rep, csts=csts, scst=scst, idn=idn)
    in_maps = []
    for c in range(NCORES):
        m = dict(shared)
        m["x"] = np.ascontiguousarray(x[BPC * c: BPC * (c + 1)].reshape(BPC * SEQ, D))
        in_maps.append(m)

    res = run_bass_kernel_spmd(nc, in_maps, core_ids=list(range(NCORES)))
    out = np.concatenate([np.asarray(res.results[c]["o"]).reshape(BPC, SEQ, D)
                          for c in range(NCORES)], axis=0)
    return out


# version 22 (final)
# speedup vs baseline: 1.0131x; 1.0131x over previous
import numpy as np
from contextlib import ExitStack

import concourse.bass as bass
import concourse.mybir as mybir
from concourse.bass_utils import run_bass_kernel_spmd

F32 = mybir.dt.float32
BF16 = mybir.dt.bfloat16
U8 = mybir.dt.uint8
AF = mybir.ActivationFunctionType
ALU = mybir.AluOpType

NCORES = 8
BPC = 4
SEQ = 2048
D = 1024
E2 = 2048
F1 = 4096
TPB = 512
NCH = 4
CH = 128
NPASS = BPC * (SEQ // TPB)
DKT = D // 128
EKT = E2 // 128
F1T = F1 // 128
CIW = TPB + 3
DBG = 0


class Ctr:
    __slots__ = ("h", "n")

    def __init__(self, h):
        self.h = h
        self.n = 0


class Slot:
    __slots__ = ("free",)

    def __init__(self):
        self.free = []

    def waits(self):
        return list(self.free)


def build_program():
    nc = bass.Bass("TRN2", target_bir_lowering=False)

    dx = nc.dram_tensor("x", [BPC * SEQ, D], F32, kind="ExternalInput").ap()
    dwinTr = nc.dram_tensor("winTr", [8, 128, 4096], F32, kind="ExternalInput").ap()
    dk0tr = nc.dram_tensor("k0tr", [8, 128, 4096], U8, kind="ExternalInput").ap()
    dk0br = nc.dram_tensor("k0br", [8, 128, 4096], U8, kind="ExternalInput").ap()
    dwoutTr = nc.dram_tensor("woutTr", [4, 128, 4096], F32, kind="ExternalInput").ap()
    dk1r = nc.dram_tensor("k1r", [32, 128, 4096], U8, kind="ExternalInput").ap()
    dk2r = nc.dram_tensor("k2r", [8, 128, 4096], U8, kind="ExternalInput").ap()
    dwxT = nc.dram_tensor("wxT", [E2, 32], F32, kind="ExternalInput").ap()
    dbtlog = nc.dram_tensor("btlog", [16, E2], F32, kind="ExternalInput").ap()
    dconvw = nc.dram_tensor("convw", [128, 64], F32, kind="ExternalInput").ap()
    dconvb = nc.dram_tensor("convb", [128, 16], F32, kind="ExternalInput").ap()
    ds012 = nc.dram_tensor("s012", [1, 3], F32, kind="ExternalInput").ap()
    dlng = nc.dram_tensor("lng_rep", [128, D], F32, kind="ExternalInput").ap()
    dlnb = nc.dram_tensor("lnb_rep", [128, D], F32, kind="ExternalInput").ap()
    dcsts = nc.dram_tensor("csts", [128, 4], F32, kind="ExternalInput").ap()
    dscst = nc.dram_tensor("scst", [1, 384], F32, kind="ExternalInput").ap()
    didn = nc.dram_tensor("idn", [128, 128], F32, kind="ExternalInput").ap()
    dout = nc.dram_tensor("o", [BPC * SEQ, D], F32, kind="ExternalOutput").ap()

    winTs = nc.dram_tensor("winTs", [8, 128, 4096], BF16, kind="Internal").ap()
    w0s = nc.dram_tensor("w0s", [8, 128, 4096], BF16, kind="Internal").ap()
    w1s = nc.dram_tensor("w1s", [32, 128, 4096], BF16, kind="Internal").ap()
    woutTs = nc.dram_tensor("woutTs", [4, 128, 4096], BF16, kind="Internal").ap()
    w2s = nc.dram_tensor("w2s", [8, 128, 4096], BF16, kind="Internal").ap()

    with ExitStack() as stack:
        e = stack.enter_context
        sbt = lambda name, shape, dt: e(nc.sbuf_tensor(name, shape, dt))
        id_sb = sbt("id_sb", [128, 128], F32)
        csts = sbt("csts_sb", [128, 4], F32)
        scst = sbt("scst_sb", [1, 384], F32)
        s012 = sbt("s012_sb", [1, 3], F32)
        wxT = sbt("wxT_sb", [128, EKT * 32], BF16)
        bt = sbt("bt_sb", [16, E2], BF16)
        convw = sbt("convw_sb", [128, 64], F32)
        convb = sbt("convb_sb", [128, 16], F32)
        lng = sbt("lng_sb", [128, D], F32)
        lnb = sbt("lnb_sb", [128, D], F32)
        scales = sbt("scales_sb", [128, 3], F32)
        decay = sbt("decay_sb", [128, TPB], F32)
        state = sbt("state_sb", [128, EKT], F32)
        hist = sbt("hist_sb", [128, EKT * 3], BF16)
        x_sb = sbt("x_sb", [128, NCH * D], F32)
        xntok = sbt("xntok_sb", [128, NCH * D], F32)
        xnfm = sbt("xnfm_sb", [128, DKT * TPB], BF16)
        a0 = sbt("a0_sb", [128, F1T * TPB], BF16)
        a1 = sbt("a1_sb", [128, F1T * TPB], BF16)
        h_sb = sbt("h_sb", [128, 2 * TPB], F32)
        hct = sbt("hct_sb", [128, TPB], F32)
        acc = sbt("acc_sb", [128, 2 * TPB], F32)
        sC = sbt("sC_sb", [16, TPB], BF16)
        stok = sbt("stok_sb", [128, NCH * D], BF16)
        resid = sbt("resid_sb", [128, 2 * D], F32)
        snorm = sbt("snorm_sb", [128, D], F32)
        g_sb = sbt("g_sb", [128, 2 * D], BF16)
        out_sb = sbt("out_sb", [128, 2 * D], F32)
        bn12 = sbt("bn12_sb", [128, 12], F32)
        bn2 = sbt("bn2_sb", [128, 2], F32)
        lnv = sbt("lnv_sb", [128, 1], F32)
        rstA = sbt("rstA_sb", [128, NCH], F32)
        nmrA = sbt("nmrA_sb", [128, NCH], F32)
        rstD = sbt("rstD_sb", [128, 1], F32)
        nmrD = sbt("nmrD_sb", [128, 1], F32)
        y_sb = sbt("y_sb", [128, EKT * TPB], BF16)
        wstr = sbt("wstr_sb", [128, 3 * 4096], BF16)
        pd = [e(nc.psum_tensor(f"pd{i}", [128, 1024], F32)) for i in range(4)]

        mksem = lambda name: Ctr(e(nc.semaphore(name)))
        svv = mksem("svv")
        sss = mksem("sss")
        spe = mksem("spe")
        wch = [mksem(f"wch{i}") for i in range(3)]
        wgh = [mksem(f"wgh{i}") for i in range(3)]
        xch = [mksem(f"xch{i}") for i in range(NCH)]
        x2c = [mksem(f"x2c{i}") for i in range(NCH)]
        och = [mksem(f"och{i}") for i in range(2)]
        gwc = [mksem(f"gwc{i}") for i in range(3)]
        gsk = mksem("gsk")
        wbk = {f: mksem(f"wbk{f}") for f in "BEDFG"}
        gmisc = mksem("gmisc")

        block = e(nc.Block())
        Q = {k: [] for k in "TVSYG"}

        def mk_dma(eng, ctr):
            def emit(build, waits=()):
                w = [(c.h, v) for (c, v) in waits if v > 0]
                ctr.n += 16
                hh0 = ctr.h

                def fn(engine):
                    for hh, vv_ in w:
                        engine.wait_ge(hh, vv_)
                    build(engine).then_inc(hh0, 16)
                Q[eng].append(fn)
                return ctr.n
            return emit

        def mk_serial(eng, ctr):
            def emit(build, waits=()):
                prev = ctr.n
                w = [(c.h, v) for (c, v) in waits if v > 0]
                ctr.n += 1
                hh0 = ctr.h

                def fn(engine):
                    if prev > 0:
                        engine.wait_ge(hh0, prev)
                    for hh, vv_ in w:
                        engine.wait_ge(hh, vv_)
                    build(engine).then_inc(hh0, 1)
                Q[eng].append(fn)
                return ctr.n
            return emit

        V = mk_serial("V", svv)
        S = mk_serial("S", sss)
        YDc = [mk_dma("Y", c) for c in wch]
        SDc = [mk_dma("S", c) for c in xch]
        X2c = [mk_dma("S", c) for c in x2c]
        ODc = [mk_dma("S", c) for c in och]
        GW = [mk_dma("G", c) for c in gwc]
        EY = [mk_dma("Y", c) for c in wch]
        EG = [mk_dma("G", c) for c in wgh]
        GSK = mk_dma("G", gsk)
        WB = {f: mk_dma("Y", c) for f, c in wbk.items()}
        GD = mk_dma("G", gmisc)

        def T(builds, waits=()):
            w = [(c.h, v) for (c, v) in waits if v > 0]
            spe.n += 1
            hh0 = spe.h

            def fn(t):
                for hh, vv_ in w:
                    t.wait_ge(hh, vv_)
                inst = None
                for b in builds:
                    inst = b(t)
                inst.then_inc(hh0, 1)
            Q["T"].append(fn)
            return spe.n

        def yv(dt, c=None):
            if c is None:
                return y_sb[:, dt * TPB:(dt + 1) * TPB]
            return y_sb[:, dt * TPB + c * CH: dt * TPB + (c + 1) * CH]

        def zv(dt):
            return a0[:, EKT * TPB + dt * TPB: EKT * TPB + (dt + 1) * TPB]

        def civ(dt, lo, hi):
            return a1[:, dt * CIW + lo: dt * CIW + hi]

        def a1t(k, c):
            return a1[:, k * TPB + c * CH: k * TPB + (c + 1) * CH]

        def wslot(j):
            return wstr[:, j * 4096:(j + 1) * 4096]

        def wslot_kt(j, kt):
            return wstr[:, j * 4096:(j + 1) * 4096].rearrange(
                "p (kt m) -> p kt m", kt=kt)

        xnfm_v = xnfm[:].rearrange("p (kt t) -> p kt t", kt=DKT)
        civ_all = a1[:, 0:EKT * CIW].rearrange("p (dt w) -> p dt w", w=CIW)
        hist_v = hist[:].rearrange("p (dt w) -> p dt w", w=3)
        pd2_kt = pd[2][:].rearrange("p (kt m) -> p kt m", kt=DKT)

        rotAB = [Slot() for _ in range(4)]
        rotC = [Slot() for _ in range(2)]
        slot3 = Slot()
        pdFree = [Slot() for _ in range(4)]
        wstrS = [Slot() for _ in range(3)]
        accS = [Slot() for _ in range(2)]
        gS = [Slot() for _ in range(2)]
        outS = [Slot() for _ in range(2)]
        residS = [Slot() for _ in range(2)]
        xchunk = [Slot() for _ in range(NCH)]
        xn_tok_free = [Slot() for _ in range(NCH)]
        xnfm_free = Slot()
        a1_freeV = Slot()
        outW = [None, None]

        def rotab_pd(m):
            j = m % 4
            return (pd[j // 2][:, (j % 2) * 512:(j % 2) * 512 + 512], rotAB[j])

        WFAM = {"B": (dwinTr, winTs), "E": (dk0tr, w0s), "D": (dwoutTr, woutTs),
                "F": (dk1r, w1s), "G": (dk2r, w2s)}
        WBTOT = {"B": 8 * 16, "E": 8 * 16, "D": 4 * 16, "F": 32 * 16, "G": 8 * 16}
        k0sub_prev = [0]
        wbprev = {f: 0 for f in "BEDFG"}
        gslot = [0]

        def wload(P, fam, idx, first):
            gi = gslot[0]
            gslot[0] += 1
            sl = gi % 3
            orig, scratch = WFAM[fam]
            if P >= 1:
                w = wstrS[sl].waits()
                if P == 1 and first:
                    w = w + [(wbk[fam], WBTOT[fam])]
                emit_q, ctr = (EY[sl], wch[sl]) if gi % 2 == 0 else (EG[sl], wgh[sl])
                lw = emit_q(lambda y, sl=sl, scratch=scratch, idx=idx:
                            y.dma_start(wslot(sl), scratch[idx]), waits=w)
                return sl, [(ctr, lw)], []
            lw = GW[sl](lambda g, sl=sl, orig=orig, idx=idx:
                        g.dma_start(wslot(sl), orig[idx]), waits=wstrS[sl].waits())
            if fam == "E":
                lb = GSK(lambda g, idx=idx: g.dma_start(stok[:], dk0br[idx]),
                         waits=([(svv, k0sub_prev[0])] if k0sub_prev[0] else []))
                tv = V(lambda v, sl=sl: v.tensor_tensor(
                    wslot(sl), wslot(sl), stok[:], ALU.subtract),
                    waits=[(gwc[sl], lw), (gsk, lb)])
                k0sub_prev[0] = tv
                mmdep = [(svv, tv)]
            elif fam in ("F", "G"):
                tv = V(lambda v, sl=sl: v.tensor_scalar(
                    wslot(sl), wslot(sl), 2.0, -1.0, ALU.mult, ALU.add),
                    waits=[(gwc[sl], lw)])
                mmdep = [(svv, tv)]
            else:
                mmdep = [(gwc[sl], lw)]
            wprev = wbprev[fam]
            wb = WB[fam](lambda y, scratch=scratch, idx=idx, sl=sl:
                         y.dma_start(scratch[idx], wslot(sl)),
                         waits=mmdep + ([(wbk[fam], wprev)] if wprev else []))
            wbprev[fam] = wb
            return sl, mmdep, [(wbk[fam], wb)]

        GD(lambda g: g.dma_start(id_sb[:], didn))
        GD(lambda g: g.dma_start(csts[:], dcsts))
        GD(lambda g: g.dma_start(scst[:], dscst))
        ld_s012 = GD(lambda g: g.dma_start(s012[:], ds012))
        GD(lambda g: g.dma_start(convw[:], dconvw))
        GD(lambda g: g.dma_start(convb[:], dconvb))
        GD(lambda g: g.dma_start(lng[:], dlng))
        GD(lambda g: g.dma_start(lnb[:], dlnb))
        GD(lambda g: g.dma_start(
            wxT[:].rearrange("p (dt j) -> p dt j", dt=EKT),
            dwxT.rearrange("(dt p) j -> p dt j", p=128)))
        ld_bt = GD(lambda g: g.dma_start(out_sb[0:16, 0:E2], dbtlog))
        V(lambda v: v.memset(decay[:], 0.95))
        bt_done = S(lambda s: s.activation(bt[:], out_sb[0:16, 0:E2], AF.Exp),
                    waits=[(gmisc, ld_bt)])
        t_sc = T([(lambda t, i=i: t.matmul(pd[2][:, i:i + 1],
                                           scst[0:1, i * 128:(i + 1) * 128],
                                           s012[0:1, i:i + 1], start=True, stop=True))
                  for i in range(3)],
                 waits=[(gmisc, ld_bt)])
        sc_done = V(lambda v: v.tensor_copy(scales[:], pd[2][:, 0:3]), waits=[(spe, t_sc)])
        rotC[0].free = [(svv, sc_done)]
        rotC[1].free = [(svv, sc_done)]


        def emit_A_loads(P):
            row0 = P * TPB
            lxs = []
            for c in range(NCH):
                lxs.append(SDc[c](lambda s, c=c, r0=row0: s.dma_start(
                    x_sb[:, c * D:(c + 1) * D], dx[r0 + c * CH: r0 + (c + 1) * CH, :]),
                    waits=xchunk[c].waits()))
            return lxs

        def emit_A_stats(lxs):
            vals = []
            for c in range(NCH):
                V(lambda v, c=c: v.bn_stats(bn12[:, 0:6], x_sb[:, c * D:c * D + 512]),
                  waits=[(xch[c], lxs[c])])
                V(lambda v, c=c: v.bn_stats(bn12[:, 6:12], x_sb[:, c * D + 512:(c + 1) * D]))
                v2 = V(lambda v: v.bn_aggr(bn2[:], bn12[:]))
                S(lambda s: s.activation(lnv[:], bn2[:, 1:2], AF.Ln, bias=csts[:, 0:1]),
                  waits=[(svv, v2)])
                s2 = S(lambda s, c=c: s.activation(rstA[:, c:c + 1], lnv[:], AF.Exp,
                                                   scale=csts[:, 1:2]))
                v3 = V(lambda v, c=c: v.tensor_scalar(nmrA[:, c:c + 1], bn2[:, 0:1],
                                                      rstA[:, c:c + 1], -1.0,
                                                      ALU.mult, ALU.mult),
                       waits=[(sss, s2)])
                vals.append(v3)
            return vals

        def emit_A_s3(c, vals):
            s3 = S(lambda s, c=c: s.activation(
                xntok[:, c * D:(c + 1) * D], x_sb[:, c * D:(c + 1) * D], AF.Identity,
                scale=rstA[:, c:c + 1], bias=nmrA[:, c:c + 1]),
                waits=[(svv, vals[c])] + xn_tok_free[c].waits())
            xchunk[c].free = [(sss, s3)]
            return s3

        def emit_A_transp(c, s3):
            tg = T([(lambda t, c=c, j=j: t.transpose(
                pd[2][:, j * 128:(j + 1) * 128],
                xntok[:, c * D + j * 128: c * D + (j + 1) * 128], id_sb[:]))
                for j in range(DKT)],
                waits=[(sss, s3)] + slot3.waits())
            xn_ready = S(lambda s, c=c: s.activation(
                xnfm_v[:, :, c * CH:(c + 1) * CH], pd2_kt, AF.Identity),
                waits=[(spe, tg)] + (xnfm_free.waits() if c == 0 else []))
            slot3.free = [(sss, xn_ready)]
            return xn_ready

        def emit_A_apply(vals):
            for c in range(NCH):
                xn_ready = emit_A_transp(c, emit_A_s3(c, vals))
            return xn_ready

        slot3.free = [(svv, sc_done)]
        xn_ready = emit_A_apply(emit_A_stats(emit_A_loads(0)))
        V(lambda v: v.memset(hist[:], 0.0))
        hci_cur = V(lambda v: v.tensor_copy(civ_all[:, :, 0:3], hist_v))
        pending_st = []
        for P in range(NPASS):
            batch_start = (P % (SEQ // TPB) == 0)
            row0 = P * TPB
            lxs_next = emit_A_loads(P + 1) if P + 1 < NPASS else None

            hci = hci_cur
            convs = [0] * EKT
            zsilu_last = 0
            pend = {}
            for m in range(32):
                sb_i = m // 4
                j = m % 4
                if j == 0:
                    sl, mmw, fre = wload(P, "B", sb_i, first=(sb_i == 0))
                pslot, prot = rotab_pd(m)
                tg = T([(lambda t, sl=sl, j=j, k=k, pslot=pslot: t.matmul(
                    pslot, wslot_kt(sl, DKT)[:, k, j * 128:(j + 1) * 128],
                    xnfm_v[:, k, :],
                    start=(k == 0), stop=(k == DKT - 1)))
                    for k in range(DKT)],
                    waits=mmw + [(sss, xn_ready)] + prot.waits())
                if j == 3:
                    wstrS[sl].free = [(spe, tg)] + fre
                if m - 2 in pend:
                    m2, asl2, ho2 = pend.pop(m - 2)
                    convs[m2] = S(lambda s, m2=m2, asl2=asl2: s.activation(
                        civ(m2, 3, CIW), acc[:, asl2 * TPB:(asl2 + 1) * TPB], AF.Silu),
                        waits=[(svv, ho2)])
                    accS[asl2].free = [(sss, convs[m2])]
                if 1 <= m <= 2 and pending_st:
                    pending_st.pop(0)()
                if m < EKT:
                    ev = S(lambda s, m=m, pslot=pslot: s.copy(civ(m, 3, CIW), pslot),
                           waits=[(spe, tg), (svv, hci)])
                    prot.free = [(sss, ev)]
                    asl = m % 2
                    V(lambda v, m=m, asl=asl: v.tensor_scalar(
                        acc[:, asl * TPB:(asl + 1) * TPB], civ(m, 0, TPB),
                        convw[:, m * 4: m * 4 + 1], convb[:, m:m + 1], ALU.mult, ALU.add),
                        waits=[(sss, ev)] + accS[asl].waits())
                    for kk in range(1, 4):
                        V(lambda v, m=m, asl=asl, kk=kk: v.scalar_tensor_tensor(
                            acc[:, asl * TPB:(asl + 1) * TPB], civ(m, kk, TPB + kk),
                            convw[:, m * 4 + kk: m * 4 + kk + 1],
                            acc[:, asl * TPB:(asl + 1) * TPB], ALU.mult, ALU.add))
                    ho = V(lambda v, m=m: v.tensor_copy(
                        hist[:, m * 3:(m + 1) * 3], civ(m, TPB, CIW)))
                    pend[m] = (m, asl, ho)
                else:
                    zsilu_last = S(lambda s, m=m, pslot=pslot: s.activation(
                        zv(m - EKT), pslot, AF.Silu),
                        waits=[(spe, tg)])
                    prot.free = [(sss, zsilu_last)]
            tg_sp = T([(lambda t, k=k: t.matmul(
                pd[2][0:32, 0:512], wxT[:, k * 32:(k + 1) * 32], civ(k, 3, CIW),
                start=(k == 0), stop=(k == EKT - 1)))
                for k in range(EKT)],
                waits=[(sss, convs[EKT - 1]), (sss, bt_done)] + rotC[0].waits())
            sC_ev = S(lambda s: s.copy(sC[:], pd[2][0:16, 0:512]), waits=[(spe, tg_sp)])
            rotC[0].free = [(sss, sC_ev)]

            y_vals = [0] * EKT
            a0_last = 0
            lw = 0
            sl = 0
            for i in range(16):
                dt = i
                hsl = dt % 2
                init = 0.0 if batch_start else state[:, dt:dt + 1]
                V(lambda v, dt=dt, hsl=hsl, init=init: v.tensor_tensor_scan(
                    h_sb[:, hsl * TPB:(hsl + 1) * TPB], decay[:], civ(dt, 3, CIW),
                    init, ALU.mult, ALU.add),
                    waits=[(sss, convs[dt])])
                V(lambda v, dt=dt, hsl=hsl: v.tensor_copy(
                    state[:, dt:dt + 1], h_sb[:, hsl * TPB + TPB - 1: hsl * TPB + TPB]))
                csl = dt % 2
                cps = pd[2][:, csl * 512:(csl + 1) * 512]
                tg_c = T([(lambda t, dt=dt, cps=cps: t.matmul(
                    cps, bt[:, dt * 128:(dt + 1) * 128], sC[:], start=True, stop=True))],
                    waits=[(sss, sC_ev)] + rotC[csl].waits())
                V(lambda v, dt=dt, hsl=hsl: v.tensor_tensor(
                    hct[:], h_sb[:, hsl * TPB:(hsl + 1) * TPB], zv(dt), ALU.mult),
                    waits=([(sss, zsilu_last)] if dt == 0 else []))
                y_vals[dt] = V(lambda v, dt=dt, cps=cps: v.tensor_tensor(
                    yv(dt), hct[:], cps, ALU.mult),
                    waits=[(spe, tg_c)])
                rotC[csl].free = [(svv, y_vals[dt])]
                for m in (2 * i, 2 * i + 1):
                    sb_i = m // 4
                    j = m % 4
                    if j == 0:
                        sl, mmw, fre = wload(P, "E", sb_i, first=(sb_i == 0))
                    pslot, prot = rotab_pd(m)
                    tg = T([(lambda t, sl=sl, j=j, k=k, pslot=pslot: t.matmul(
                        pslot, wslot_kt(sl, DKT)[:, k, j * 128:(j + 1) * 128],
                        xnfm_v[:, k, :],
                        start=(k == 0), stop=(k == DKT - 1)))
                        for k in range(DKT)],
                        waits=mmw + prot.waits())
                    if j == 3:
                        wstrS[sl].free = [(spe, tg)] + fre
                    extra = [(svv, y_vals[m - EKT])] if m >= EKT else []
                    a0_last = S(lambda s, m=m, pslot=pslot: s.activation(
                        a0[:, m * TPB:(m + 1) * TPB], pslot, AF.Relu, scale=scales[:, 0:1]),
                        waits=[(spe, tg)] + extra)
                    prot.free = [(sss, a0_last)]
            tg_e_last = tg
            xnfm_free.free = [(spe, tg_e_last)]
            x2v = [0] * NCH
            x2v[2] = X2c[2](lambda v, r0=row0: v.dma_start(
                h_sb[:], dx[r0 + 2 * CH: r0 + 3 * CH, :]),
                waits=[(svv, y_vals[EKT - 1])])
            x2v[3] = X2c[3](lambda v, r0=row0: v.dma_start(
                acc[:], dx[r0 + 3 * CH: r0 + 4 * CH, :]),
                waits=[(sss, convs[EKT - 1])])
            a_vals = emit_A_stats(lxs_next) if lxs_next is not None else None

            for k in range(EKT):
                sb_i = k // 4
                j = k % 4
                if j == 0:
                    sl, mmw, fre = wload(P, "D", sb_i, first=(sb_i == 0))
                extra = [(svv, y_vals[EKT - 1]), (sss, a0_last)] if k == 0 else []
                builds = []
                for c in range(NCH):
                    for hh in range(2):
                        builds.append(lambda t, k=k, c=c, hh=hh, sl=sl, j=j: t.matmul(
                            pd[c][:, hh * 512:(hh + 1) * 512], yv(k, c),
                            wslot(sl)[:, j * 1024 + hh * 512: j * 1024 + (hh + 1) * 512],
                            start=(k == 0), stop=(k == EKT - 1), skip_group_check=True))
                tg_d = T(builds, waits=mmw + extra)
                if j == 3:
                    wstrS[sl].free = [(spe, tg_d)] + fre
            resid_vals = [0] * NCH
            rloc = [resid[:, 0:D], resid[:, D:2 * D], out_sb[:, 0:D], out_sb[:, D:2 * D]]
            for c in range(NCH):
                ww = residS[c].waits() if c < 2 else list(outS[c - 2].waits())
                rv = V(lambda v, c=c: v.tensor_tensor(
                    rloc[c], pd[c][:], xntok[:, c * D:(c + 1) * D], ALU.add),
                    waits=[(spe, tg_d)] + ww)
                resid_vals[c] = rv
                pdFree[c].free = [(svv, rv)]
                xn_tok_free[c].free = [(svv, rv)]

            slot3.free = [(svv, resid_vals[2])]
            if P + 1 < NPASS:
                s3_next = [emit_A_s3(c, a_vals) for c in range(NCH)]

            s3_c0 = 0
            ln_last = 0
            for c in range(NCH):
                V(lambda v, c=c: v.bn_stats(bn12[:, 0:6], rloc[c][:, 0:512]))
                V(lambda v, c=c: v.bn_stats(bn12[:, 6:12], rloc[c][:, 512:D]))
                v2 = V(lambda v: v.bn_aggr(bn2[:], bn12[:]))
                S(lambda s: s.activation(lnv[:], bn2[:, 1:2], AF.Ln, bias=csts[:, 0:1]),
                  waits=[(svv, v2)])
                s2 = S(lambda s: s.activation(rstD[:], lnv[:], AF.Exp, scale=csts[:, 1:2]))
                v3 = V(lambda v: v.tensor_scalar(nmrD[:], bn2[:, 0:1], rstD[:], -1.0,
                                                 ALU.mult, ALU.mult),
                       waits=[(sss, s2)])
                s3 = S(lambda s, c=c: s.activation(
                    snorm[:], rloc[c], AF.Identity, scale=rstD[:], bias=nmrD[:]),
                    waits=[(svv, v3)])
                if c == 0:
                    s3_c0 = s3
                if c < 2:
                    residS[c].free = [(sss, s3)]
                else:
                    outW[c - 2] = (sss, s3)
                V(lambda v: v.tensor_tensor(snorm[:], snorm[:], lng[:], ALU.mult),
                  waits=[(sss, s3)])
                ln_last = V(lambda v, c=c: v.tensor_tensor(
                    stok[:, c * D:(c + 1) * D], snorm[:], lnb[:], ALU.add))

            a1_vals = [0] * 32
            for m in range(32):
                sl, mmw, fre = wload(P, "F", m, first=(m == 0))
                pslot, prot = rotab_pd(m)
                pw = list(prot.waits())
                if m < 4:
                    pw += pdFree[[0, 0, 1, 1][m]].waits()
                tg = T([(lambda t, sl=sl, k=k, pslot=pslot: t.matmul(
                    pslot, wslot_kt(sl, F1T)[:, k, :], a0[:, k * TPB:(k + 1) * TPB],
                    start=(k == 0), stop=(k == F1T - 1)))
                    for k in range(F1T)],
                    waits=mmw + [(sss, a0_last)] + pw)
                wstrS[sl].free = [(spe, tg)] + fre
                extra = [(svv, y_vals[EKT - 1])] if m == 0 else []
                a1_vals[m] = S(lambda s, m=m, pslot=pslot: s.activation(
                    a1[:, m * TPB:(m + 1) * TPB], pslot, AF.Relu, scale=scales[:, 1:2]),
                    waits=[(spe, tg)] + extra)
                prot.free = [(sss, a1_vals[m])]
                if P + 1 < NPASS and m in (3, 7, 11, 15):
                    xn_ready = emit_A_transp((m - 3) // 4, s3_next[(m - 3) // 4])
                if m == 8:
                    x2v[0] = X2c[0](lambda s, r0=row0: s.dma_start(
                        snorm[:], dx[r0: r0 + CH, :]), waits=[(svv, ln_last)])
                elif m == 9:
                    x2v[1] = X2c[1](lambda s, r0=row0: s.dma_start(
                        resid[:, 0:D], dx[r0 + CH: r0 + 2 * CH, :]),
                        waits=[(sss, s3_c0)])

            for k in range(F1T):
                sb_i = k // 4
                j = k % 4
                if j == 0:
                    sl, mmw, fre = wload(P, "G", sb_i, first=(sb_i == 0))
                extra = [(sss, a1_vals[31]), (svv, resid_vals[3])] if k == 0 else []
                if k == 0 and P + 1 < NPASS:
                    extra = extra + [(sss, xn_ready)]
                builds = []
                for c in range(NCH):
                    for hh in range(2):
                        builds.append(lambda t, k=k, c=c, hh=hh, sl=sl, j=j: t.matmul(
                            pd[c][:, hh * 512:(hh + 1) * 512], a1t(k, c),
                            wslot(sl)[:, j * 1024 + hh * 512: j * 1024 + (hh + 1) * 512],
                            start=(k == 0), stop=(k == F1T - 1), skip_group_check=True))
                tg_g = T(builds, waits=mmw + extra)
                if j == 3:
                    wstrS[sl].free = [(spe, tg_g)] + fre
                if k == 16 and P + 1 < NPASS:
                    if (P + 1) % (SEQ // TPB) == 0:
                        V(lambda v: v.memset(hist[:], 0.0), waits=[(spe, tg_g)])
                        hci_cur = V(lambda v: v.tensor_copy(civ_all[:, :, 0:3], hist_v))
                    else:
                        hci_cur = V(lambda v: v.tensor_copy(civ_all[:, :, 0:3], hist_v),
                                    waits=[(spe, tg_g)])
            gvs = [0] * NCH
            for c in (0, 1):
                gvs[c] = S(lambda s, c=c: s.activation(
                    g_sb[:, (c % 2) * D:(c % 2 + 1) * D], pd[c][:], AF.Silu,
                    scale=scales[:, 2:3]),
                    waits=[(spe, tg_g)] + gS[c % 2].waits())
            rotAB[0].free = [(sss, gvs[0])]
            rotAB[1].free = [(sss, gvs[0])]
            rotAB[2].free = [(sss, gvs[1])]
            rotAB[3].free = [(sss, gvs[1])]
            for c in range(NCH):
                gsl = c % 2
                if c >= 2:
                    gvs[c] = S(lambda s, c=c, gsl=gsl: s.activation(
                        g_sb[:, gsl * D:(gsl + 1) * D], pd[c][:], AF.Silu,
                        scale=scales[:, 2:3]),
                        waits=[(spe, tg_g)] + gS[gsl].waits())
                    if c == 2:
                        rotC[0].free = [(sss, gvs[2])]
                        rotC[1].free = [(sss, gvs[2])]
                gv = gvs[c]
                osl = c % 2
                oww = [outW[osl]] if outW[osl] else []
                if DBG == 6:
                    u1 = V(lambda v, osl=osl: v.memset(
                        out_sb[:, osl * D:(osl + 1) * D], 7.0),
                        waits=[(sss, gv)] + outS[osl].waits() + oww)
                    u2 = u1
                elif DBG == 1:
                    u1 = V(lambda v, c=c, osl=osl: v.tensor_copy(
                        out_sb[:, osl * D:(osl + 1) * D], xntok[:, c * D:(c + 1) * D]),
                        waits=[(sss, gv)] + outS[osl].waits() + oww)
                    u2 = u1
                elif DBG == 3:
                    u1 = V(lambda v, c=c, osl=osl: v.tensor_copy(
                        out_sb[:, osl * D:(osl + 1) * D], stok[:, c * D:(c + 1) * D]),
                        waits=[(sss, gv)] + outS[osl].waits() + oww)
                    u2 = u1
                elif DBG == 4:
                    u1 = V(lambda v, gsl=gsl, osl=osl: v.tensor_copy(
                        out_sb[:, osl * D:(osl + 1) * D], g_sb[:, gsl * D:(gsl + 1) * D]),
                        waits=[(sss, gv)] + outS[osl].waits() + oww)
                    u2 = u1
                else:
                    xsrc = [snorm[:], resid[:, 0:D], h_sb[:], acc[:]][c]
                    u1 = V(lambda v, c=c, gsl=gsl, osl=osl: v.tensor_tensor(
                        out_sb[:, osl * D:(osl + 1) * D], stok[:, c * D:(c + 1) * D],
                        g_sb[:, gsl * D:(gsl + 1) * D], ALU.mult),
                        waits=[(sss, gv)] + outS[osl].waits() + oww)
                    u2 = V(lambda v, osl=osl, xsrc=xsrc: v.tensor_tensor(
                        out_sb[:, osl * D:(osl + 1) * D], out_sb[:, osl * D:(osl + 1) * D],
                        xsrc, ALU.add),
                        waits=[(x2c[c], x2v[c])])
                outW[osl] = None
                gS[gsl].free = [(svv, u1)]

                def emit_store(c=c, osl=osl, r0=row0, u2=u2):
                    ov = ODc[osl](lambda s, c=c, osl=osl, r0=r0: s.dma_start(
                        dout[r0 + c * CH: r0 + (c + 1) * CH, :],
                        out_sb[:, osl * D:(osl + 1) * D]),
                        waits=[(svv, u2)])
                    outS[osl].free = [(och[osl], ov)]
                if c < 2:
                    emit_store()
                else:
                    pending_st.append(emit_store)
            a1_freeV.free = [(svv, y_vals[EKT - 1])]

        for st in pending_st:
            st()
        pending_st.clear()

        def fin(s):
            s.wait_ge(och[0].h, och[0].n)
            s.wait_ge(och[1].h, och[1].n)
        Q["S"].append(fin)

        @block.tensor
        def _(t):
            for fn in Q["T"]:
                fn(t)

        @block.vector
        def _(v):
            for fn in Q["V"]:
                fn(v)

        @block.scalar
        def _(s):
            for fn in Q["S"]:
                fn(s)

        @block.sync
        def _(y):
            for fn in Q["Y"]:
                fn(y)

        @block.gpsimd
        def _(g):
            for fn in Q["G"]:
                fn(g)

    return nc


_CACHE = {}


def kernel(L0_attn_x, L0_attn_W_in, L0_attn_conv_w, L0_attn_conv_b, L0_attn_W_x,
           L0_attn_B_log, L0_attn_W_out, L0_attn_ln_g, L0_attn_ln_b,
           L0_ffn_k0, L0_ffn_k1, L0_ffn_k2, L0_ffn_s0, L0_ffn_s1, L0_ffn_s2):
    x = np.ascontiguousarray(np.asarray(L0_attn_x, np.float32))
    W_in = np.asarray(L0_attn_W_in, np.float32)
    conv_w = np.asarray(L0_attn_conv_w, np.float32)
    conv_b = np.asarray(L0_attn_conv_b, np.float32)
    W_x = np.asarray(L0_attn_W_x, np.float32)
    B_log = np.asarray(L0_attn_B_log, np.float32)
    W_out = np.asarray(L0_attn_W_out, np.float32)
    ln_g = np.asarray(L0_attn_ln_g, np.float32)
    ln_b = np.asarray(L0_attn_ln_b, np.float32)
    k0 = np.ascontiguousarray(np.asarray(L0_ffn_k0)).view(np.uint8)
    k1 = np.ascontiguousarray(np.asarray(L0_ffn_k1)).view(np.uint8)
    k2 = np.ascontiguousarray(np.asarray(L0_ffn_k2)).view(np.uint8)

    winT = W_in.T
    winTr = np.ascontiguousarray(
        winT.reshape(8, 128, 8, 512).transpose(2, 1, 0, 3).reshape(8, 128, 4096))
    k0tr = np.ascontiguousarray(
        k0[:1024].reshape(8, 128, 8, 512).transpose(2, 1, 0, 3).reshape(8, 128, 4096))
    k0br = np.ascontiguousarray(
        k0[1024:].reshape(8, 128, 8, 512).transpose(2, 1, 0, 3).reshape(8, 128, 4096))
    woutTr = np.ascontiguousarray(
        W_out.T.reshape(4, 4, 128, 1024).transpose(0, 2, 1, 3).reshape(4, 128, 4096))
    k1r = np.ascontiguousarray(
        k1.reshape(32, 128, 32, 128).transpose(2, 1, 0, 3).reshape(32, 128, 4096))
    k2r = np.ascontiguousarray(
        k2.reshape(8, 4, 128, 1024).transpose(0, 2, 1, 3).reshape(8, 128, 4096))
    wxT = np.ascontiguousarray(
        np.concatenate([W_x[16:32], W_x[0:16]], axis=0).T)
    btlog = np.ascontiguousarray(B_log.T)
    convw = np.ascontiguousarray(
        conv_w[:, 0, :].reshape(16, 128, 4).transpose(1, 0, 2).reshape(128, 64))
    convb = np.ascontiguousarray(conv_b.reshape(16, 128).T)
    s012 = np.stack([np.asarray(L0_ffn_s0, np.float32).reshape(()),
                     np.asarray(L0_ffn_s1, np.float32).reshape(()),
                     np.asarray(L0_ffn_s2, np.float32).reshape(())]).reshape(1, 3)
    lng_rep = np.ascontiguousarray(np.broadcast_to(ln_g[None, :], (128, 1024)))
    lnb_rep = np.ascontiguousarray(np.broadcast_to(ln_b[None, :], (128, 1024)))
    csts = np.ascontiguousarray(np.broadcast_to(
        np.array([[1e-5, -0.5, 0.0, 0.0]], np.float32), (128, 4)))
    scst = np.zeros((1, 384), np.float32)
    scst[0, 0:128] = 4.0
    scst[0, 128:256] = 2.0
    scst[0, 256:384] = 1.0
    idn = np.eye(128, dtype=np.float32)

    key = ("nc", DBG)
    if key not in _CACHE:
        _CACHE[key] = build_program()
    nc = _CACHE[key]

    shared = dict(winTr=winTr, k0tr=k0tr, k0br=k0br, woutTr=woutTr, k1r=k1r,
                  k2r=k2r, wxT=wxT, btlog=btlog, convw=convw,
                  convb=convb, s012=s012, lng_rep=lng_rep,
                  lnb_rep=lnb_rep, csts=csts, scst=scst, idn=idn)
    in_maps = []
    for c in range(NCORES):
        m = dict(shared)
        m["x"] = np.ascontiguousarray(x[BPC * c: BPC * (c + 1)].reshape(BPC * SEQ, D))
        in_maps.append(m)

    res = run_bass_kernel_spmd(nc, in_maps, core_ids=list(range(NCORES)))
    out = np.concatenate([np.asarray(res.results[c]["o"]).reshape(BPC, SEQ, D)
                          for c in range(NCORES)], axis=0)
    return out
